# version 17
# baseline (speedup 1.0000x reference)
import numpy as np
import concourse.bass as bass
import concourse.mybir as mybir
from concourse.bass_utils import run_bass_kernel_spmd

F32 = mybir.dt.float32
BF16 = mybir.dt.bfloat16
AX = mybir.AxisListType.X
AXC = mybir.AxisListType.C
ALU = mybir.AluOpType
ACT = mybir.ActivationFunctionType

MAGIC = 12582912.0
EPS = 1e-5
INV127 = float(np.float32(1.0) / np.float32(127.0))
E, T, D, HID = 1024, 2048, 64, 4096
ECH, HCH, NB = 8, 32, 4
HM2 = 2 * HCH
TBW, TBN = 512, 4

C_INV1, C_C1, C_INVW1, C_INVW2, C_INV2, C_C2 = 0, 1, 2, 3, 4, 5
S_XMAX, S_W1MAX, S_W2MAX, S_HMAX = 0, 1, 2, 3
WC_W1, WC_W2, WC_W1M, WC_W2M, WC_XN, WC_XNM = 0, 16, 48, 49, 50, 58

SEM_NAMES = [
    "initd", "ldx", "ldqkv", "ldb",
    "ldw1a0", "ldw1a1", "vw1a", "w1mg",
    "lw1b0", "lw1b1", "lw1b2", "w1r", "w1rG", "w1q",
    "ldw2a0", "ldw2a1", "vw2a", "w2mg",
    "sq", "accS", "accQ0", "accQ1", "stat", "var", "se", "xn", "hi",
    "xnm", "xmr", "cc1d",
    "w16", "qmm", "qev", "kmm", "kev", "vmm", "vev",
    "scal1", "sw1v", "bcAmm", "bc1", "vacc", "xq",
    "smm", "bcBmm", "bcB", "expd", "vs", "pv", "sev", "s2bd",
    "f1", "gel", "spl0", "spl1", "spl2", "spl3", "habs",
    "sw2v", "bcWmm", "bcw", "hmg", "cc2d",
    "scal2", "bc2mm", "bc2", "b2c",
    "lw2b0", "lw2b1", "lw2b2", "lw2b3", "w2q",
    "ldh0", "ldh1", "ldh2", "ldh3", "ldh4", "ldh5", "ldh6", "ldh7",
    "hq1", "hq", "mmk", "sil", "ldxr", "ep",
    "outd0", "outd1", "outd2", "outd3", "outd4", "outd5", "outd6", "outd7",
    "vself", "vself2",
]


def build_nc():
    from contextlib import ExitStack
    nc = bass.Bass("TRN2", target_bir_lowering=False,
                   detect_race_conditions=False)

    xT = nc.dram_tensor("xT", [E, T], F32, kind="ExternalInput")
    wqd = nc.dram_tensor("wq", [E, D], F32, kind="ExternalInput")
    wkd = nc.dram_tensor("wk", [E, D], F32, kind="ExternalInput")
    wvd = nc.dram_tensor("wv", [E, D], F32, kind="ExternalInput")
    w1t = nc.dram_tensor("w1t", [E, HID], F32, kind="ExternalInput")
    w2t = nc.dram_tensor("w2t", [HID, E], F32, kind="ExternalInput")
    b1r = nc.dram_tensor("b1r", [128, HID // 128], F32, kind="ExternalInput")
    b2r = nc.dram_tensor("b2r", [128, E // 128], F32, kind="ExternalInput")
    outT = nc.dram_tensor("outT", [E, T], F32, kind="ExternalOutput")
    hsp = nc.dram_tensor("hsp", [HID, T], F32)
    cc1i = nc.dram_tensor("cc1i", [1, 16], F32)
    cc1o = nc.dram_tensor("cc1o", [1, 16], F32)
    cc2i = nc.dram_tensor("cc2i", [1, 16], F32)
    cc2o = nc.dram_tensor("cc2o", [1, 16], F32)

    es = ExitStack()
    with es:
        def sb(name, shape, dt, side="left"):
            return es.enter_context(nc.sbuf_tensor(name, shape, dt, side=side))

        ones = sb("ones", [128, 128], F32)
        sb1 = sb("sb1", [128, 32], F32)
        b1c = sb("b1c", [128, 32], F32)
        sb2 = sb("sb2", [128, 8], F32)
        b2c = sb("b2c", [128, 8], F32)
        scal = sb("scal", [128, 16], F32)
        wmx = sb("wmx", [128, 64], F32)
        sstat = sb("sstat", [1, 16], F32)
        sgl = sb("sgl", [1, 16], F32)
        sgl2 = sb("sgl2", [1, 16], F32)
        swq = sb("swq", [128, 512], F32)
        swk = sb("swk", [128, 512], F32)
        swv = sb("swv", [128, 512], F32)
        swq6 = sb("swq6", [128, 512], BF16)
        swk6 = sb("swk6", [128, 512], BF16)
        swv6 = sb("swv6", [128, 512], BF16)
        qT16 = sb("qT16", [64, T], BF16)
        kT16 = sb("kT16", [64, T], BF16)
        vnat = sb("vnat", [128, 1024], F32)
        vs16 = sb("vs16", [128, 1024], BF16)
        dsum = sb("dsum", [128, 16], F32)
        dinv = sb("dinv", [128, 16], F32)
        astat = sb("astat", [128, 4], F32)
        nmag = sb("nmag", [128, 1], F32)
        pmag = sb("pmag", [128, 1], F32)
        epsb = sb("epsb", [128, 1], F32)
        s2b = sb("s2b", [128, T], F32)
        wstage = sb("wstage", [128, 2 * 2048], F32, side="right")

        nc.const_aps.register(F32, -MAGIC, nmag[:, 0:1])
        nc.const_aps.register(F32, MAGIC, pmag[:, 0:1])
        nc.const_aps.register(F32, EPS, epsb[:, 0:1])
        sm = {n: es.enter_context(nc.semaphore(n)) for n in SEM_NAMES}

        vself_n = [0]
        vself2_n = [0]

        def mkguard(v, sem, ctr):
            def guard(ins):
                ctr[0] += 1
                ins.then_inc(sem, 1)
                v.wait_ge(sem, ctr[0])
            return guard

        xq_es = ExitStack()
        xq16 = xq_es.enter_context(
            nc.sbuf_tensor("xq16", [128, ECH * T], BF16, side="right"))

        def w1_src(c):
            k, half = (c, 0) if c < 8 else (c - 8, 1)
            return k, half

        with (
            nc.sbuf_tensor("xbuf", [128, ECH * T], F32) as xbuf,
            nc.sbuf_tensor("hi16", [128, ECH * T], BF16) as hi16,
            nc.sbuf_tensor("accs", [128, T], F32, side="right") as accs,
            nc.sbuf_tensor("accq", [128, T], F32, side="right") as accq,
            nc.sbuf_tensor("sqt", [128, T], F32, side="right") as sqt,
            nc.Block() as blk,
        ):
            @blk.sync
            def _(sy):
                for k in range(ECH):
                    sy.dma_start(out=xbuf[:, k * T:(k + 1) * T],
                                 in_=xT[k * 128:(k + 1) * 128, :]
                                 ).then_inc(sm["ldx"], 16)
                for wsb, wdr in ((swq, wqd), (swk, wkd), (swv, wvd)):
                    for k in range(ECH):
                        sy.dma_start(out=wsb[:, k * 64:(k + 1) * 64],
                                     in_=wdr[k * 128:(k + 1) * 128, :]
                                     ).then_inc(sm["ldqkv"], 16)
                sy.dma_start(out=sb1[:, :], in_=b1r[:, :]).then_inc(sm["ldb"], 16)
                sy.dma_start(out=sb2[:, :], in_=b2r[:, :]).then_inc(sm["ldb"], 16)
                for c in range(12):
                    if c >= 2:
                        sy.wait_ge(sm["vw1a"], c - 1)
                    sy.dma_start(
                        out=wstage[:, (c % 2) * 2048:(c % 2 + 1) * 2048],
                        in_=w1t[(c // 2) * 128:(c // 2) * 128 + 128,
                                (c % 2) * 2048:(c % 2 + 1) * 2048]
                    ).then_inc(sm["ldw1a%d" % (c % 2)], 16)
            @blk.scalar
            def _(s):
                s.wait_ge(sm["ldx"], 8 * 16)
                for k in range(ECH):
                    if k >= 2:
                        s.wait_ge(sm["sq"], 2 * k - 1)
                    slot = sqt if k % 2 == 0 else accs
                    s.activation(slot[:, :], xbuf[:, k * T:(k + 1) * T],
                                 ACT.Square).then_inc(sm["sq"], 1)
                s.wait_ge(sm["var"], 1)
                s.activation(sqt[:, :], accq[:, :], ACT.Sqrt,
                             bias=EPS).then_inc(sm["se"], 1)

            @blk.gpsimd
            def _(g):
                g.wait_ge(sm["xnm"], 1)
                g.tensor_reduce(sstat[0:1, S_XMAX:S_XMAX + 1],
                                wmx[:, WC_XNM:WC_XNM + 1], AXC, ALU.max
                                ).then_inc(sm["xmr"], 1)
                g.wait_ge(sm["xmr"], 1)
                g.dma_start(out=cc1i[:, :], in_=sstat[:, :]
                            ).then_inc(sm["cc1d"], 16)
                g.wait_ge(sm["cc1d"], 16)
                g.collective_compute(
                    "AllReduce", ALU.max, replica_groups=[list(range(8))],
                    ins=[cc1i.ap().opt()], outs=[cc1o.ap().opt()]
                ).then_inc(sm["cc1d"], 1)
            with (
                nc.psum_tensor("psA", [128, T], F32) as psA,
                nc.psum_tensor("psB", [128, T], F32) as psB,
            ):
                @blk.tensor
                def _(t):
                    t.wait_ge(sm["ldx"], 8 * 16)
                    t.wait_ge(sm["initd"], 1)
                    for k in range(ECH):
                        for nb in range(NB):
                            t.matmul(psA[:, nb * 512:(nb + 1) * 512],
                                     ones[:, :],
                                     xbuf[:, k * T + nb * 512:
                                          k * T + (nb + 1) * 512],
                                     start=(k == 0), stop=(k == ECH - 1))
                        t.wait_ge(sm["sq"], 2 * k + 1)
                        slot = sqt if k % 2 == 0 else accs
                        for nb in range(NB):
                            ins = t.matmul(psB[:, nb * 512:(nb + 1) * 512],
                                           ones[:, :],
                                           slot[:, nb * 512:(nb + 1) * 512],
                                           start=(k == 0), stop=(k == ECH - 1))
                        ins.then_inc(sm["sq"], 1)
                    t.wait_ge(sm["sq"], 2 * ECH)
                    t.sem_inc(sm["stat"], 1)

                @blk.vector
                def _(v):
                    g1 = mkguard(v, sm["vself"], vself_n)
                    v.memset(ones[:, :], 1.0)
                    v.memset(nmag[:, :], -MAGIC)
                    v.memset(pmag[:, :], MAGIC)
                    v.memset(epsb[:, :], EPS)
                    v.memset(sstat[:, :], 0.0)
                    v.memset(dsum[:, :], 0.0)
                    v.memset(sgl2[:, :], 0.0).then_inc(sm["initd"], 1)
                    v.wait_ge(sm["ldqkv"], 24 * 16)
                    v.tensor_copy(swq6[:, :], swq[:, :])
                    v.tensor_copy(swk6[:, :], swk[:, :])
                    v.tensor_copy(swv6[:, :], swv[:, :]).then_inc(sm["w16"], 1)
                    for c in range(10):
                        v.wait_ge(sm["ldw1a%d" % (c % 2)], (c // 2 + 1) * 16)
                        v.reduce_max(wmx[:, WC_W1 + c:WC_W1 + c + 1],
                                     wstage[:, (c % 2) * 2048:(c % 2 + 1) * 2048],
                                     axis=AX, apply_absolute_value=True
                                     ).then_inc(sm["vw1a"], 1)
                    v.wait_ge(sm["stat"], 1)
                    v.tensor_scalar(accs[:, :], psA[:, :], 1.0 / 1024.0, None,
                                    ALU.mult)
                    v.tensor_scalar(accq[:, :], psB[:, :], 1.0 / 1024.0, None,
                                    ALU.mult)
                    v.tensor_tensor(sqt[:, :], accs[:, :], accs[:, :], ALU.mult)
                    v.tensor_tensor(accq[:, :], accq[:, :], sqt[:, :],
                                    ALU.subtract).then_inc(sm["var"], 1)
                    v.wait_ge(sm["se"], 1)
                    v.reciprocal(accq[:, :], sqt[:, :])
                    v.scalar_tensor_tensor(accs[:, :], accs[:, :], -1.0,
                                           accq[:, :], ALU.mult, ALU.mult)
                    for k in range(ECH):
                        v.tensor_tensor(xbuf[:, k * T:(k + 1) * T],
                                        xbuf[:, k * T:(k + 1) * T],
                                        accq[:, :], ALU.mult)
                        v.tensor_tensor(xbuf[:, k * T:(k + 1) * T],
                                        xbuf[:, k * T:(k + 1) * T],
                                        accs[:, :], ALU.add
                                        ).then_inc(sm["xn"], 1)
                    for k in range(ECH):
                        ins = v.reduce_max(wmx[:, WC_XN + k:WC_XN + k + 1],
                                           xbuf[:, k * T:(k + 1) * T], axis=AX,
                                           apply_absolute_value=True)
                    g1(ins)
                    v.reduce_max(wmx[:, WC_XNM:WC_XNM + 1],
                                 wmx[:, WC_XN:WC_XN + ECH], axis=AX
                                 ).then_inc(sm["xnm"], 1)

                @blk.scalar
                def _(s):
                    for k in range(ECH):
                        s.wait_ge(sm["xn"], k + 1)
                        s.copy(hi16[:, k * T:(k + 1) * T],
                               xbuf[:, k * T:(k + 1) * T]).then_inc(sm["hi"], 1)

            with (
                nc.psum_tensor("psQ", [128, T], F32) as psQ,
                nc.psum_tensor("psV", [128, 1024], F32) as psV,
            ):
                @blk.tensor
                def _(t):
                    t.wait_ge(sm["w16"], 1)
                    for k in range(ECH):
                        t.wait_ge(sm["hi"], k + 1)
                        for nb in range(NB):
                            ins = t.matmul(
                                psQ[0:64, nb * 512:(nb + 1) * 512],
                                swq6[:, k * 64:(k + 1) * 64],
                                hi16[:, k * T + nb * 512:k * T + (nb + 1) * 512],
                                start=(k == 0), stop=(k == ECH - 1))
                    ins.then_inc(sm["qmm"], 1)

                @blk.scalar
                def _(s):
                    s.wait_ge(sm["qmm"], 1)
                    s.copy(qT16[:, :], psQ[0:64, :]).then_inc(sm["qev"], 1)

                @blk.tensor
                def _(t):
                    t.wait_ge(sm["qev"], 1)
                    for nb in range(NB):
                        for k in range(ECH):
                            ins = t.matmul(
                                psQ[0:64, nb * 512:(nb + 1) * 512],
                                swk6[:, k * 64:(k + 1) * 64],
                                hi16[:, k * T + nb * 512:k * T + (nb + 1) * 512],
                                start=(k == 0), stop=(k == ECH - 1))
                    ins.then_inc(sm["kmm"], 1)

                @blk.scalar
                def _(s):
                    s.wait_ge(sm["kmm"], 1)
                    s.copy(kT16[:, :], psQ[0:64, :]).then_inc(sm["kev"], 1)

                @blk.tensor
                def _(t):
                    for tc in range(16):
                        for k in range(ECH):
                            ins = t.matmul(
                                psV[:, tc * 64:(tc + 1) * 64],
                                hi16[:, k * T + tc * 128:k * T + (tc + 1) * 128],
                                swv6[:, k * 64:(k + 1) * 64],
                                start=(k == 0), stop=(k == ECH - 1))
                    ins.then_inc(sm["vmm"], 1)

                @blk.scalar
                def _(s):
                    s.wait_ge(sm["vmm"], 1)
                    s.copy(vnat[:, :], psV[:, :]).then_inc(sm["vev"], 1)

        with (
            nc.sbuf_tensor("hst", [128, 4 * 1024], F32) as hst,
            nc.sbuf_tensor("habs", [128, 66], F32) as habs,
            nc.sbuf_tensor("w1q", [128, ECH * HID], BF16) as w1q,
            nc.sbuf_tensor("sTt", [64, T], F32) as sTt,
            nc.sbuf_tensor("pt16", [128, 2 * T], BF16) as pt16,
            nc.sbuf_tensor("wrndG", [128, 2048], F32) as wrndG,
            nc.sbuf_tensor("wrnd", [128, 2048], F32, side="right") as wrnd,
            nc.Block() as blk,
        ):
            def w1slot(j):
                if j % 3 < 2:
                    return wstage[:, (j % 3) * 2048:(j % 3 + 1) * 2048]
                return accq[:, :]

            @blk.sync
            def _(sy):
                for c in range(12, 16):
                    sy.wait_ge(sm["vw1a"], c - 1)
                    sy.dma_start(
                        out=wstage[:, (c % 2) * 2048:(c % 2 + 1) * 2048],
                        in_=w1t[(c // 2) * 128:(c // 2) * 128 + 128,
                                (c % 2) * 2048:(c % 2 + 1) * 2048]
                    ).then_inc(sm["ldw1a%d" % (c % 2)], 16)
                for j in range(16):
                    k, half = w1_src(j)
                    if j % 3 < 2:
                        sy.wait_ge(sm["vw1a"], 15 + (j % 3))
                    else:
                        sy.wait_ge(sm["xn"], ECH)
                    if 3 <= j <= 10:
                        sy.wait_ge(sm["w1r"], j - 2)
                    elif j >= 11:
                        sy.wait_ge(sm["w1rG"], j - 10)
                    sy.dma_start(
                        out=w1slot(j),
                        in_=w1t[k * 128:(k + 1) * 128,
                                half * 2048:(half + 1) * 2048]
                    ).then_inc(sm["lw1b%d" % (j % 3)], 16)
                for c in range(16):
                    sy.wait_ge(sm["w1rG"], 8 - 2 * (c % 2))
                    if c >= 2:
                        sy.wait_ge(sm["vw2a"], c - 1)
                    sy.dma_start(
                        out=wstage.ap().rearrange("p (s q j) -> p s q j",
                                                  s=2, q=2)[:, c % 2, :, :],
                        in_=w2t[256 * c:256 * (c + 1), :].rearrange(
                            "(q p) j -> p q j", p=128)
                    ).then_inc(sm["ldw2a%d" % (c % 2)], 16)

            with (
                nc.psum_tensor("psS", [128, T], F32) as psS,
                nc.psum_tensor("psPV", [128, T], F32) as psPV,
            ):
                @blk.tensor
                def _(t):
                    t.wait_ge(sm["kev"], 1)
                    for kc in range(16):
                        if kc == 9:
                            t.wait_ge(sm["expd"], 4 * kc)
                            t.wait_ge(sm["scal1"], 1)
                            t.wait_ge(sm["sw1v"], 1)
                            t.matmul(psS[:, 0:16], ones[0:1, :], sgl2[0:1, :],
                                     start=True, stop=True
                                     ).then_inc(sm["bcAmm"], 1)
                            t.wait_ge(sm["bc1"], 1)
                        for ib in range(NB):
                            if kc > 0:
                                t.wait_ge(sm["expd"], 4 * (kc - 1) + ib + 1)
                            t.matmul(psS[:, ib * 512:(ib + 1) * 512],
                                     kT16[:, kc * 128:(kc + 1) * 128],
                                     qT16[:, ib * 512:(ib + 1) * 512],
                                     start=True, stop=True
                                     ).then_inc(sm["smm"], 1)
                        if kc > 0:
                            t.wait_ge(sm["vs"], kc)
                            for ib in range(NB):
                                ins = t.matmul(
                                    psPV[0:64, ib * 512:(ib + 1) * 512],
                                    vs16[:, (kc - 1) * 64:kc * 64],
                                    pt16[:, ((kc - 1) % 2) * T + ib * 512:
                                         ((kc - 1) % 2) * T + (ib + 1) * 512],
                                    start=(kc == 1), stop=False)
                            ins.then_inc(sm["pv"], 1)
                    t.wait_ge(sm["vs"], 16)
                    for ib in range(NB):
                        ins = t.matmul(psPV[0:64, ib * 512:(ib + 1) * 512],
                                       vs16[:, 15 * 64:16 * 64],
                                       pt16[:, T + ib * 512:T + (ib + 1) * 512],
                                       start=False, stop=True)
                    ins.then_inc(sm["pv"], 1)

                @blk.vector
                def _(v):
                    g1 = mkguard(v, sm["vself"], vself_n)
                    g2 = mkguard(v, sm["vself2"], vself2_n)

                    def attn(kc):
                        v.wait_ge(sm["expd"], 4 * (kc + 1))
                        if kc == 0:
                            v.wait_ge(sm["vev"], 1)
                        g1(v.reduce_sum(dsum[:, kc:kc + 1],
                                        pt16[:, (kc % 2) * T:(kc % 2 + 1) * T],
                                        axis=AX))
                        g1(v.reciprocal(dinv[:, kc:kc + 1], dsum[:, kc:kc + 1]))
                        v.tensor_scalar(vs16[:, kc * 64:(kc + 1) * 64],
                                        vnat[:, kc * 64:(kc + 1) * 64],
                                        dinv[:, kc:kc + 1], None, ALU.mult
                                        ).then_inc(sm["vs"], 1)

                    def w1a(c):
                        v.wait_ge(sm["ldw1a%d" % (c % 2)], (c // 2 + 1) * 16)
                        v.reduce_max(wmx[:, WC_W1 + c:WC_W1 + c + 1],
                                     wstage[:, (c % 2) * 2048:(c % 2 + 1) * 2048],
                                     axis=AX, apply_absolute_value=True
                                     ).then_inc(sm["vw1a"], 1)

                    def xq1(k):
                        if k >= 1:
                            v.wait_ge(sm["xq"], k)
                        v.tensor_scalar(sqt[:, :],
                                        xbuf[:, k * T:(k + 1) * T],
                                        scal[:, C_INV1:C_INV1 + 1], MAGIC,
                                        ALU.mult, ALU.add
                                        ).then_inc(sm["vacc"], 1)

                    def w1q1(c):
                        v.wait_ge(sm["lw1b%d" % (c % 3)], (c // 3 + 1) * 16)
                        if c >= 3:
                            v.wait_ge(sm["w1q"], c - 2)
                        wr = (wrnd[:, :] if c % 3 == 0 else
                              hst[:, (c % 3 - 1) * 2048:(c % 3) * 2048])
                        v.tensor_scalar(wr, w1slot(c),
                                        scal[:, C_INVW1:C_INVW1 + 1], MAGIC,
                                        ALU.mult, ALU.add
                                        ).then_inc(sm["w1r"], 1)

                    attn(0)
                    w1a(10); w1a(11)
                    attn(1)
                    w1a(12); w1a(13)
                    attn(2)
                    w1a(14); w1a(15)
                    v.wait_ge(sm["vw1a"], 16)
                    v.reduce_max(wmx[:, WC_W1M:WC_W1M + 1],
                                 wmx[:, WC_W1:WC_W1 + 16], axis=AX
                                 ).then_inc(sm["vw1a"], 1)
                    attn(3); attn(4); attn(5)
                    v.wait_ge(sm["w1mg"], 1)
                    g2(v.tensor_scalar(sgl2[0:1, 9:10],
                                       sstat[0:1, S_W1MAX:S_W1MAX + 1],
                                       1e-6, INV127, ALU.max, ALU.mult))
                    g2(v.reciprocal(sgl2[0:1, C_INVW1:C_INVW1 + 1],
                                    sgl2[0:1, 9:10]))
                    v.sem_inc(sm["sw1v"], 1)
                    attn(6); attn(7); attn(8)
                    v.wait_ge(sm["cc1d"], 33)
                    g2(v.tensor_scalar(sgl2[0:1, 8:9],
                                       sgl[0:1, S_XMAX:S_XMAX + 1],
                                       1e-6, INV127, ALU.max, ALU.mult))
                    g2(v.reciprocal(sgl2[0:1, C_INV1:C_INV1 + 1],
                                    sgl2[0:1, 8:9]))
                    g2(v.tensor_tensor(sgl2[0:1, C_C1:C_C1 + 1], sgl2[0:1, 8:9],
                                       sgl2[0:1, 9:10], ALU.mult))
                    v.sem_inc(sm["scal1"], 1)
                    attn(9)
                    v.wait_ge(sm["bc1"], 1)
                    v.tensor_scalar(b1c[:, :], sb1[:, :],
                                    scal[:, C_C1:C_C1 + 1], None, ALU.mult)
                    attn(10)
                    xq1(0)
                    w1q1(0)
                    attn(11)
                    xq1(1); xq1(2)
                    w1q1(1)
                    attn(12)
                    xq1(3); xq1(4)
                    w1q1(2)
                    attn(13)
                    xq1(5); xq1(6)
                    w1q1(3)
                    attn(14)
                    xq1(7)
                    w1q1(4)
                    attn(15)
                    for c in range(5, 8):
                        w1q1(c)

                @blk.scalar
                def _(s):
                    def expk(kc):
                        if kc >= 2:
                            s.wait_ge(sm["pv"], kc - 1)
                        for ib in range(NB):
                            s.wait_ge(sm["smm"], 4 * kc + ib + 1)
                            s.activation(
                                pt16[:, (kc % 2) * T + ib * 512:
                                     (kc % 2) * T + (ib + 1) * 512],
                                psS[:, ib * 512:(ib + 1) * 512], ACT.Exp
                            ).then_inc(sm["expd"], 1)

                    def xq2(k):
                        s.wait_ge(sm["vacc"], k + 1)
                        s.activation(xq16[:, k * T:(k + 1) * T], sqt[:, :],
                                     ACT.Copy, bias=-MAGIC
                                     ).then_inc(sm["xq"], 1)

                    def w1q2(c):
                        k, half = w1_src(c)
                        if c < 8:
                            s.wait_ge(sm["w1r"], c + 1)
                            wr = (wrnd[:, :] if c % 3 == 0 else
                                  hst[:, (c % 3 - 1) * 2048:(c % 3) * 2048])
                        else:
                            s.wait_ge(sm["w1rG"], c - 7)
                            wr = wrndG[:, :]
                        s.wait_ge(sm["vacc"], min(k + 3 + half, ECH))
                        s.activation(
                            w1q[:, k * HID + half * 2048:
                                k * HID + (half + 1) * 2048],
                            wr,
                            ACT.Copy, bias=-MAGIC
                        ).then_inc(sm["w1q"], 1)

                    s.wait_ge(sm["vmm"], 1)
                    for kc in range(9):
                        expk(kc)
                    s.wait_ge(sm["bcAmm"], 1)
                    s.wait_ge(sm["ldb"], 32)
                    s.copy(scal[:, 0:3], psS[:, 0:3]).then_inc(sm["bc1"], 1)
                    expk(9); expk(10)
                    xq2(0)
                    expk(11)
                    xq2(1); xq2(2)
                    w1q2(0)
                    expk(12)
                    xq2(3); xq2(4)
                    w1q2(1)
                    expk(13)
                    xq2(5); xq2(6)
                    w1q2(2)
                    expk(14)
                    xq2(7)
                    w1q2(3)
                    expk(15)
                    for c in range(4, 8):
                        w1q2(c)
                    s.wait_ge(sm["pv"], 16)
                    s.copy(sTt[:, :], psPV[0:64, :]).then_inc(sm["sev"], 1)

                @blk.gpsimd
                def _(g):
                    g.wait_ge(sm["vw1a"], 17)
                    g.tensor_reduce(sstat[0:1, S_W1MAX:S_W1MAX + 1],
                                    wmx[:, WC_W1M:WC_W1M + 1], AXC, ALU.max
                                    ).then_inc(sm["w1mg"], 1)
                    g.wait_ge(sm["cc1d"], 17)
                    g.dma_start(out=sgl[:, :], in_=cc1o[:, :]
                                ).then_inc(sm["cc1d"], 16)
                    g.wait_ge(sm["sev"], 1)
                    g.dma_start(out=s2b[0:64, :], in_=sTt[:, :]
                                ).then_inc(sm["s2bd"], 16)
                    g.dma_start(out=s2b[64:128, :], in_=sTt[:, :]
                                ).then_inc(sm["s2bd"], 16)

            with nc.psum_tensor("psF", [128, 4 * 1024], F32) as psF:
                @blk.tensor
                def _(t):
                    t.wait_ge(sm["xq"], ECH)
                    for hm in range(HM2):
                        m, half = hm // 2, hm % 2
                        if hm == 0:
                            t.wait_ge(sm["expd"], 64)
                            t.wait_ge(sm["w1q"], 8)
                        if hm == 2:
                            t.wait_ge(sm["sev"], 1)
                        if hm == 32:
                            t.wait_ge(sm["w1q"], 16)
                        if hm >= 4:
                            t.wait_ge(sm["gel"], hm - 3)
                        pf = psF[:, (hm % 4) * 1024:(hm % 4 + 1) * 1024]
                        for nb2 in range(2):
                            for k in range(ECH):
                                ins = t.matmul(
                                    pf[:, nb2 * 512:(nb2 + 1) * 512],
                                    w1q[:, k * HID + m * 128:
                                        k * HID + (m + 1) * 128],
                                    xq16[:, k * T + half * 1024 + nb2 * 512:
                                         k * T + half * 1024 + (nb2 + 1) * 512],
                                    start=(k == 0), stop=(k == ECH - 1))
                        ins.then_inc(sm["f1"], 1)
                    t.wait_ge(sm["gel"], HM2)
                    t.wait_ge(sm["sw2v"], 1)
                    t.matmul(psF[:, 0:16], ones[0:1, :], sgl2[0:1, :],
                             start=True, stop=True).then_inc(sm["bcWmm"], 1)
                    t.wait_ge(sm["scal2"], 1)
                    t.matmul(psF[:, 1024:1040], ones[0:1, :], sgl2[0:1, :],
                             start=True, stop=True).then_inc(sm["bc2mm"], 1)

                @blk.scalar
                def _(s):
                    def w1q2G(c):
                        k, half = w1_src(c)
                        s.wait_ge(sm["w1rG"], c - 7)
                        s.activation(
                            w1q[:, k * HID + half * 2048:
                                k * HID + (half + 1) * 2048],
                            wrndG[:, :],
                            ACT.Copy, bias=-MAGIC
                        ).then_inc(sm["w1q"], 1)

                    for hm in range(HM2):
                        m = hm // 2
                        s.wait_ge(sm["f1"], hm + 1)
                        if hm >= 4:
                            s.wait_ge(sm["spl%d" % (hm % 4)],
                                      (hm // 4) * 16)
                            s.wait_ge(sm["habs"], hm - 3)
                        pf = psF[:, (hm % 4) * 1024:(hm % 4 + 1) * 1024]
                        hs = hst[:, (hm % 4) * 1024:(hm % 4 + 1) * 1024]
                        for nb2 in range(2):
                            ins = s.activation(hs[:, nb2 * 512:(nb2 + 1) * 512],
                                               pf[:, nb2 * 512:(nb2 + 1) * 512],
                                               ACT.Gelu, bias=b1c[:, m:m + 1],
                                               scale=scal[:, C_C1:C_C1 + 1])
                        ins.then_inc(sm["gel"], 1)
                        if hm % 2 == 1 and hm < 16:
                            w1q2G(8 + hm // 2)
                    s.wait_ge(sm["bcWmm"], 1)
                    s.copy(scal[:, C_INVW2:C_INVW2 + 1],
                           psF[:, 3:4]).then_inc(sm["bcw"], 1)
                    s.wait_ge(sm["vw2a"], 17)
                    for c in range(4):
                        s.dma_start(
                            out=wstage[:, c * 1024:(c + 1) * 1024],
                            in_=w2t[c * 128:(c + 1) * 128, :]
                        ).then_inc(sm["lw2b%d" % c], 16)
                    s.wait_ge(sm["bc2mm"], 1)
                    s.copy(scal[:, C_INV2:C_C2 + 1],
                           psF[:, 1024 + 4:1024 + 6]).then_inc(sm["bc2"], 1)

                @blk.vector
                def _(v):
                    g1 = mkguard(v, sm["vself"], vself_n)
                    g2 = mkguard(v, sm["vself2"], vself2_n)
                    for hm in range(HM2):
                        v.wait_ge(sm["gel"], hm + 1)
                        ins = v.reduce_max(habs[:, hm:hm + 1],
                                           hst[:, (hm % 4) * 1024:
                                               (hm % 4 + 1) * 1024],
                                           axis=AX, apply_absolute_value=True)
                        ins.then_inc(sm["habs"], 1)
                        if 16 <= hm < 48 and hm % 2 == 0:
                            c = (hm - 16) // 2
                            v.wait_ge(sm["ldw2a%d" % (c % 2)],
                                      (c // 2 + 1) * 16)
                            v.reduce_max(
                                wmx[:, WC_W2 + c:WC_W2 + c + 1],
                                wstage[:, (c % 2) * 2048:(c % 2 + 1) * 2048],
                                axis=AX, apply_absolute_value=True
                            ).then_inc(sm["vw2a"], 1)
                        if hm == 49:
                            v.wait_ge(sm["vw2a"], 16)
                            v.reduce_max(wmx[:, WC_W2M:WC_W2M + 1],
                                         wmx[:, WC_W2:WC_W2 + 16], axis=AX
                                         ).then_inc(sm["vw2a"], 1)
                        if hm == 52:
                            v.wait_ge(sm["w2mg"], 1)
                            g2(v.tensor_scalar(sgl2[0:1, 10:11],
                                               sstat[0:1, S_W2MAX:S_W2MAX + 1],
                                               1e-6, INV127, ALU.max, ALU.mult))
                            g2(v.reciprocal(sgl2[0:1, C_INVW2:C_INVW2 + 1],
                                            sgl2[0:1, 10:11]))
                            v.sem_inc(sm["sw2v"], 1)
                    v.wait_ge(sm["habs"], HM2)
                    v.reduce_max(habs[:, 64:65], habs[:, 0:64], axis=AX
                                 ).then_inc(sm["habs"], 1)
                    v.wait_ge(sm["cc2d"], 33)
                    g2(v.tensor_scalar(sgl2[0:1, 11:12],
                                       sgl[0:1, S_HMAX:S_HMAX + 1],
                                       1e-6, INV127, ALU.max, ALU.mult))
                    g2(v.reciprocal(sgl2[0:1, C_INV2:C_INV2 + 1],
                                    sgl2[0:1, 11:12]))
                    g2(v.tensor_tensor(sgl2[0:1, C_C2:C_C2 + 1],
                                       sgl2[0:1, 11:12],
                                       sgl2[0:1, 10:11], ALU.mult))
                    v.sem_inc(sm["scal2"], 1)
                    v.wait_ge(sm["bc2"], 1)
                    v.tensor_scalar(b2c[:, :], sb2[:, :],
                                    scal[:, C_C2:C_C2 + 1], None, ALU.mult
                                    ).then_inc(sm["b2c"], 1)

                @blk.gpsimd
                def _(g):
                    def w1qG(c):
                        g.wait_ge(sm["lw1b%d" % (c % 3)], (c // 3 + 1) * 16)
                        if c >= 9:
                            g.wait_ge(sm["w1q"], c)
                        ws = w1slot(c)
                        g.tensor_scalar(wrndG[:, 0:1024], ws[:, 0:1024],
                                        scal[:, C_INVW1:C_INVW1 + 1], MAGIC,
                                        ALU.mult, ALU.add)
                        g.tensor_scalar(wrndG[:, 1024:2048], ws[:, 1024:2048],
                                        scal[:, C_INVW1:C_INVW1 + 1], MAGIC,
                                        ALU.mult, ALU.add
                                        ).then_inc(sm["w1rG"], 1)

                    for hm in range(HM2):
                        m, half = hm // 2, hm % 2
                        g.wait_ge(sm["gel"], hm + 1)
                        g.dma_start(out=hsp[m * 128:(m + 1) * 128,
                                            half * 1024:(half + 1) * 1024],
                                    in_=hst[:, (hm % 4) * 1024:
                                            (hm % 4 + 1) * 1024]
                                    ).then_inc(sm["spl%d" % (hm % 4)], 16)
                        if hm % 2 == 0 and hm < 16:
                            w1qG(8 + hm // 2)
                        if hm == 50:
                            g.wait_ge(sm["vw2a"], 17)
                            g.tensor_reduce(sstat[0:1, S_W2MAX:S_W2MAX + 1],
                                            wmx[:, WC_W2M:WC_W2M + 1], AXC,
                                            ALU.max).then_inc(sm["w2mg"], 1)
                    g.wait_ge(sm["habs"], HM2 + 1)
                    g.tensor_reduce(sstat[0:1, S_HMAX:S_HMAX + 1],
                                    habs[:, 64:65], AXC, ALU.max
                                    ).then_inc(sm["hmg"], 1)
                    g.wait_ge(sm["hmg"], 1)
                    g.dma_start(out=cc2i[:, :], in_=sstat[:, :]
                                ).then_inc(sm["cc2d"], 16)
                    g.wait_ge(sm["cc2d"], 16)
                    g.collective_compute(
                        "AllReduce", ALU.max, replica_groups=[list(range(8))],
                        ins=[cc2i.ap().opt()], outs=[cc2o.ap().opt()]
                    ).then_inc(sm["cc2d"], 1)
                    g.wait_ge(sm["cc2d"], 17)
                    g.dma_start(out=sgl[:, :], in_=cc2o[:, :]
                                ).then_inc(sm["cc2d"], 16)

        xq_es.close()

        with (
            nc.sbuf_tensor("w2q", [128, HCH * 1024], BF16) as w2q,
            nc.sbuf_tensor("hq16", [128, HCH * TBW], BF16) as hq16,
            nc.sbuf_tensor("gbuf", [128, 8 * TBW], F32) as gbuf,
            nc.sbuf_tensor("hrd", [128, 8 * TBW], F32) as hrd,
            nc.sbuf_tensor("xrld", [128, 8 * TBW], F32) as xrld,
            nc.sbuf_tensor("wrnd2", [128, 2048], F32, side="right") as wrnd2,
            nc.Block() as blk,
        ):
            def spl_wait(eng, tb, k):
                hm = 2 * k + tb // 2
                eng.wait_ge(sm["spl%d" % (hm % 4)], (hm // 4 + 1) * 16)

            @blk.gpsimd
            def _(g):
                g.wait_ge(sm["bcw"], 1)
                g.wait_ge(sm["f1"], HM2)
                odds = [i for i in range(TBN * HCH) if i % 2 == 1]

                def odd_ld(j):
                    jtb, jk = divmod(j, HCH)
                    spl_wait(g, jtb, jk)
                    if j >= 8:
                        g.wait_ge(sm["hq"], j - 7)
                    g.dma_start(
                        out=hrd[:, (j % 8) * TBW:(j % 8 + 1) * TBW],
                        in_=hsp[jk * 128:(jk + 1) * 128,
                                jtb * TBW:(jtb + 1) * TBW]
                    ).then_inc(sm["ldh%d" % (j % 8)], 16)

                nxt = 0
                for c in range(HCH):
                    g.wait_ge(sm["lw2b%d" % (c % 4)], (c // 4 + 1) * 16)
                    if c >= 2:
                        g.wait_ge(sm["w2q"], c - 1)
                    g.tensor_scalar(wrnd2[:, (c % 2) * 1024:(c % 2 + 1) * 1024],
                                    wstage[:, (c % 4) * 1024:(c % 4 + 1) * 1024],
                                    scal[:, C_INVW2:C_INVW2 + 1], MAGIC,
                                    ALU.mult, ALU.add
                                    ).then_inc(sm["hmg"], 1)
                    while nxt < len(odds) and odds[nxt] <= c + 8:
                        odd_ld(odds[nxt])
                        nxt += 1
                while nxt < len(odds):
                    odd_ld(odds[nxt])
                    nxt += 1

            with nc.psum_tensor("psO", [128, 8 * 512], F32) as psO:
                @blk.sync
                def _(sy):
                    for tb in range(TBN):
                        for k in range(HCH):
                            idx = tb * HCH + k
                            if idx % 2 == 1:
                                continue
                            spl_wait(sy, tb, k)
                            if idx >= 8:
                                sy.wait_ge(sm["hq"], idx - 7)
                            sy.dma_start(
                                out=hrd[:, (idx % 8) * TBW:(idx % 8 + 1) * TBW],
                                in_=hsp[k * 128:(k + 1) * 128,
                                        tb * TBW:(tb + 1) * TBW]
                            ).then_inc(sm["ldh%d" % (idx % 8)], 16)
                        if tb > 0:
                            sy.wait_ge(sm["ep"], tb * 8)
                        for g_ in range(8):
                            sy.dma_start(
                                out=xrld[:, g_ * TBW:(g_ + 1) * TBW],
                                in_=xT[g_ * 128:(g_ + 1) * 128,
                                       tb * TBW:(tb + 1) * TBW]
                            ).then_inc(sm["ldxr"], 16)

                def emit_hq(v, tb, k0, k1):
                    for k in range(k0, k1):
                        idx = tb * HCH + k
                        v.wait_ge(sm["ldh%d" % (idx % 8)], (idx // 8 + 1) * 16)
                        if idx >= 8:
                            v.wait_ge(sm["hq"], idx - 7)
                        hr = hrd[:, (idx % 8) * TBW:(idx % 8 + 1) * TBW]
                        v.tensor_scalar(hr, hr, scal[:, C_INV2:C_INV2 + 1],
                                        MAGIC, ALU.mult, ALU.add
                                        ).then_inc(sm["hq1"], 1)

                def emit_ep(v, tb, g0, g1):
                    if g0 == 0:
                        v.wait_ge(sm["ldxr"], (tb + 1) * 8 * 16)
                    for g_ in range(g0, g1):
                        v.wait_ge(sm["sil"], tb * 8 + g_ + 1)
                        v.tensor_tensor(gbuf[:, g_ * TBW:(g_ + 1) * TBW],
                                        gbuf[:, g_ * TBW:(g_ + 1) * TBW],
                                        s2b[:, tb * TBW:(tb + 1) * TBW],
                                        ALU.mult)
                        v.tensor_tensor(gbuf[:, g_ * TBW:(g_ + 1) * TBW],
                                        gbuf[:, g_ * TBW:(g_ + 1) * TBW],
                                        xrld[:, g_ * TBW:(g_ + 1) * TBW],
                                        ALU.add).then_inc(sm["ep"], 1)

                @blk.vector
                def _(v):
                    v.wait_ge(sm["b2c"], 1)
                    v.wait_ge(sm["s2bd"], 32)
                    emit_hq(v, 0, 0, HCH)
                    for tb in range(1, TBN):
                        emit_hq(v, tb, 0, 8)
                        emit_ep(v, tb - 1, 0, 4)
                        emit_hq(v, tb, 8, 16)
                        emit_ep(v, tb - 1, 4, 8)
                        emit_hq(v, tb, 16, HCH)
                    emit_ep(v, TBN - 1, 0, 8)

                @blk.tensor
                def _(t):
                    for tb in range(TBN):
                        for k in range(HCH):
                            t.wait_ge(sm["hq"], tb * HCH + k + 1)
                            if tb == 0:
                                t.wait_ge(sm["w2q"], k + 1)
                            for g_ in range(8):
                                if k == 0 and tb > 0:
                                    t.wait_ge(sm["sil"], (tb - 1) * 8 + g_ + 1)
                                ins = t.matmul(
                                    psO[:, g_ * 512:(g_ + 1) * 512],
                                    w2q[:, k * 1024 + g_ * 128:
                                        k * 1024 + (g_ + 1) * 128],
                                    hq16[:, k * TBW:(k + 1) * TBW],
                                    start=(k == 0), stop=(k == HCH - 1))
                            ins.then_inc(sm["mmk"], 1)

                def emit_hq2(s, tb, k0, k1):
                    for k in range(k0, k1):
                        idx = tb * HCH + k
                        s.wait_ge(sm["hq1"], idx + 1)
                        if tb > 0:
                            s.wait_ge(sm["mmk"], (tb - 1) * HCH + k + 1)
                        s.activation(hq16[:, k * TBW:(k + 1) * TBW],
                                     hrd[:, (idx % 8) * TBW:(idx % 8 + 1) * TBW],
                                     ACT.Copy, bias=-MAGIC
                                     ).then_inc(sm["hq"], 1)

                def emit_silu(s, tb):
                    s.wait_ge(sm["mmk"], (tb + 1) * HCH)
                    for g_ in range(8):
                        if tb > 0:
                            s.wait_ge(sm["outd%d" % g_], tb * 16)
                        s.activation(gbuf[:, g_ * TBW:(g_ + 1) * TBW],
                                     psO[:, g_ * 512:(g_ + 1) * 512],
                                     ACT.Silu, bias=b2c[:, g_:g_ + 1],
                                     scale=scal[:, C_C2:C_C2 + 1]
                                     ).then_inc(sm["sil"], 1)

                def emit_outs(s, tb):
                    for g_ in range(8):
                        s.wait_ge(sm["ep"], tb * 8 + g_ + 1)
                        s.dma_start(
                            out=outT[g_ * 128:(g_ + 1) * 128,
                                     tb * TBW:(tb + 1) * TBW],
                            in_=gbuf[:, g_ * TBW:(g_ + 1) * TBW]
                        ).then_inc(sm["outd%d" % g_], 16)

                @blk.scalar
                def _(s):
                    for i in range(4):
                        s.wait_ge(sm["spl%d" % i], 16 * 16)
                    s.wait_ge(sm["b2c"], 1)
                    for k in range(HCH):
                        s.wait_ge(sm["hmg"], k + 2)
                        if k + 4 < HCH:
                            s.dma_start(
                                out=wstage[:, ((k + 4) % 4) * 1024:
                                           ((k + 4) % 4 + 1) * 1024],
                                in_=w2t[(k + 4) * 128:(k + 5) * 128, :]
                            ).then_inc(sm["lw2b%d" % ((k + 4) % 4)], 16)
                        s.activation(w2q[:, k * 1024:(k + 1) * 1024],
                                     wrnd2[:, (k % 2) * 1024:(k % 2 + 1) * 1024],
                                     ACT.Copy, bias=-MAGIC
                                     ).then_inc(sm["w2q"], 1)
                        s.wait_ge(sm["hq1"], k + 1)
                        s.activation(hq16[:, k * TBW:(k + 1) * TBW],
                                     hrd[:, (k % 8) * TBW:(k % 8 + 1) * TBW],
                                     ACT.Copy, bias=-MAGIC
                                     ).then_inc(sm["hq"], 1)
                    for tb in range(1, TBN):
                        emit_hq2(s, tb, 0, 8)
                        emit_silu(s, tb - 1)
                        emit_outs(s, tb - 1)
                        emit_hq2(s, tb, 8, HCH)
                    emit_silu(s, TBN - 1)
                    emit_outs(s, TBN - 1)
                    for g_ in range(8):
                        s.wait_ge(sm["outd%d" % g_], TBN * 16)

    return nc


def kernel(L0_attn_x, L0_attn_Wq, L0_attn_Wk, L0_attn_Wv,
           L0_ffn_w1, L0_ffn_b1, L0_ffn_w2, L0_ffn_b2):
    x = np.asarray(L0_attn_x, dtype=np.float32)
    wq = np.ascontiguousarray(np.asarray(L0_attn_Wq, dtype=np.float32))
    wk = np.ascontiguousarray(np.asarray(L0_attn_Wk, dtype=np.float32))
    wv = np.ascontiguousarray(np.asarray(L0_attn_Wv, dtype=np.float32))
    w1t = np.ascontiguousarray(np.asarray(L0_ffn_w1, dtype=np.float32).T)
    w2t = np.ascontiguousarray(np.asarray(L0_ffn_w2, dtype=np.float32).T)
    b1r = np.ascontiguousarray(
        np.asarray(L0_ffn_b1, dtype=np.float32).reshape(32, 128).T)
    b2r = np.ascontiguousarray(
        np.asarray(L0_ffn_b2, dtype=np.float32).reshape(8, 128).T)

    nc = build_nc()
    in_maps = []
    for b in range(8):
        in_maps.append(dict(xT=np.ascontiguousarray(x[b].T), wq=wq, wk=wk,
                            wv=wv, w1t=w1t, w2t=w2t, b1r=b1r, b2r=b2r))
    res = run_bass_kernel_spmd(nc, in_maps, core_ids=list(range(8)))
    return np.stack([np.ascontiguousarray(r["outT"].T) for r in res.results])


# version 18
# speedup vs baseline: 1.0986x; 1.0986x over previous
import numpy as np
import concourse.bass as bass
import concourse.mybir as mybir
from concourse.bass_utils import run_bass_kernel_spmd

F32 = mybir.dt.float32
BF16 = mybir.dt.bfloat16
AX = mybir.AxisListType.X
AXC = mybir.AxisListType.C
ALU = mybir.AluOpType
ACT = mybir.ActivationFunctionType

MAGIC = 12582912.0
EPS = 1e-5
INV127 = float(np.float32(1.0) / np.float32(127.0))
E, T, D, HID = 1024, 2048, 64, 4096
ECH, HCH, NB = 8, 32, 4
HM2 = 2 * HCH
TBW, TBN = 512, 4

C_INV1, C_C1, C_INVW1, C_INVW2, C_INV2, C_C2 = 0, 1, 2, 3, 4, 5
S_XMAX, S_W1MAX, S_W2MAX, S_HMAX = 0, 1, 2, 3
WC_W1, WC_W2, WC_W1M, WC_W2M, WC_XN, WC_XNM = 0, 16, 48, 49, 50, 58

SEM_NAMES = [
    "initd", "ldx", "ldqkv", "ldb",
    "ldw1a0", "ldw1a1", "vw1a", "w1mg",
    "lw1b0", "lw1b1", "lw1b2", "w1r", "w1rG", "w1q",
    "ldw2a0", "ldw2a1", "vw2a", "w2mg",
    "sq", "accS", "accQ0", "accQ1", "stat", "var", "se", "xn", "hi",
    "xnm", "xmr", "cc1d", "cc1c", "cc2c",
    "w16", "qmm", "qev", "kmm", "kev", "vmm", "vev",
    "scal1", "sw1v", "bcAmm", "bc1", "vacc", "xq",
    "smm", "bcBmm", "bcB", "expd", "vs", "pv", "sev", "s2bd",
    "f1", "gel", "spl0", "spl1", "spl2", "spl3", "habs",
    "sw2v", "bcWmm", "bcw", "hmg", "cc2d",
    "scal2", "bc2mm", "bc2", "b2c",
    "lw2b0", "lw2b1", "lw2b2", "lw2b3", "w2q",
    "ldh0", "ldh1", "ldh2", "ldh3", "ldh4", "ldh5", "ldh6", "ldh7",
    "hq1", "hq", "mmk", "sil", "ldxr", "ep",
    "outd0", "outd1", "outd2", "outd3", "outd4", "outd5", "outd6", "outd7",
    "vself", "vself2",
]


def build_nc():
    from contextlib import ExitStack
    nc = bass.Bass("TRN2", target_bir_lowering=False,
                   detect_race_conditions=False)

    xT = nc.dram_tensor("xT", [E, T], F32, kind="ExternalInput")
    wqd = nc.dram_tensor("wq", [E, D], F32, kind="ExternalInput")
    wkd = nc.dram_tensor("wk", [E, D], F32, kind="ExternalInput")
    wvd = nc.dram_tensor("wv", [E, D], F32, kind="ExternalInput")
    w1t = nc.dram_tensor("w1t", [E, HID], F32, kind="ExternalInput")
    w2t = nc.dram_tensor("w2t", [HID, E], F32, kind="ExternalInput")
    b1r = nc.dram_tensor("b1r", [128, HID // 128], F32, kind="ExternalInput")
    b2r = nc.dram_tensor("b2r", [128, E // 128], F32, kind="ExternalInput")
    outT = nc.dram_tensor("outT", [E, T], F32, kind="ExternalOutput")
    hsp = nc.dram_tensor("hsp", [HID, T], F32)
    cc1i = nc.dram_tensor("cc1i", [1, 16], F32)
    cc1o = nc.dram_tensor("cc1o", [1, 16], F32)
    cc2i = nc.dram_tensor("cc2i", [1, 16], F32)
    cc2o = nc.dram_tensor("cc2o", [1, 16], F32)

    es = ExitStack()
    with es:
        def sb(name, shape, dt, side="left"):
            return es.enter_context(nc.sbuf_tensor(name, shape, dt, side=side))

        ones = sb("ones", [128, 128], F32)
        sb1 = sb("sb1", [128, 32], F32)
        b1c = sb("b1c", [128, 32], F32)
        sb2 = sb("sb2", [128, 8], F32)
        b2c = sb("b2c", [128, 8], F32)
        scal = sb("scal", [128, 16], F32)
        wmx = sb("wmx", [128, 64], F32)
        sstat = sb("sstat", [1, 16], F32)
        sgl = sb("sgl", [1, 16], F32)
        sgl2 = sb("sgl2", [1, 16], F32)
        swq = sb("swq", [128, 512], F32)
        swk = sb("swk", [128, 512], F32)
        swv = sb("swv", [128, 512], F32)
        swq6 = sb("swq6", [128, 512], BF16)
        swk6 = sb("swk6", [128, 512], BF16)
        swv6 = sb("swv6", [128, 512], BF16)
        qT16 = sb("qT16", [64, T], BF16)
        kT16 = sb("kT16", [64, T], BF16)
        vnat = sb("vnat", [128, 1024], F32)
        vs16 = sb("vs16", [128, 1024], BF16)
        dsum = sb("dsum", [128, 16], F32)
        dinv = sb("dinv", [128, 16], F32)
        astat = sb("astat", [128, 4], F32)
        nmag = sb("nmag", [128, 1], F32)
        pmag = sb("pmag", [128, 1], F32)
        epsb = sb("epsb", [128, 1], F32)
        s2b = sb("s2b", [128, T], F32)
        wstage = sb("wstage", [128, 2 * 2048], F32, side="right")

        nc.const_aps.register(F32, -MAGIC, nmag[:, 0:1])
        nc.const_aps.register(F32, MAGIC, pmag[:, 0:1])
        nc.const_aps.register(F32, EPS, epsb[:, 0:1])
        sm = {n: es.enter_context(nc.semaphore(n)) for n in SEM_NAMES}

        vself_n = [0]
        vself2_n = [0]

        def mkguard(v, sem, ctr):
            def guard(ins):
                ctr[0] += 1
                ins.then_inc(sem, 1)
                v.wait_ge(sem, ctr[0])
            return guard

        xq_es = ExitStack()
        xq16 = xq_es.enter_context(
            nc.sbuf_tensor("xq16", [128, ECH * T], BF16, side="right"))

        def w1_src(c):
            k, half = (c, 0) if c < 8 else (c - 8, 1)
            return k, half

        with (
            nc.sbuf_tensor("xbuf", [128, ECH * T], F32) as xbuf,
            nc.sbuf_tensor("hi16", [128, ECH * T], BF16) as hi16,
            nc.sbuf_tensor("accs", [128, T], F32, side="right") as accs,
            nc.sbuf_tensor("accq", [128, T], F32, side="right") as accq,
            nc.sbuf_tensor("sqt", [128, T], F32, side="right") as sqt,
            nc.Block() as blk,
        ):
            @blk.sync
            def _(sy):
                for k in range(ECH):
                    sy.dma_start(out=xbuf[:, k * T:(k + 1) * T],
                                 in_=xT[k * 128:(k + 1) * 128, :]
                                 ).then_inc(sm["ldx"], 16)
                for wsb, wdr in ((swq, wqd), (swk, wkd), (swv, wvd)):
                    for k in range(ECH):
                        sy.dma_start(out=wsb[:, k * 64:(k + 1) * 64],
                                     in_=wdr[k * 128:(k + 1) * 128, :]
                                     ).then_inc(sm["ldqkv"], 16)
                sy.dma_start(out=sb1[:, :], in_=b1r[:, :]).then_inc(sm["ldb"], 16)
                sy.dma_start(out=sb2[:, :], in_=b2r[:, :]).then_inc(sm["ldb"], 16)
                for c in range(12):
                    if c >= 2:
                        sy.wait_ge(sm["vw1a"], c - 1)
                    sy.dma_start(
                        out=wstage[:, (c % 2) * 2048:(c % 2 + 1) * 2048],
                        in_=w1t[(c // 2) * 128:(c // 2) * 128 + 128,
                                (c % 2) * 2048:(c % 2 + 1) * 2048]
                    ).then_inc(sm["ldw1a%d" % (c % 2)], 16)
            @blk.scalar
            def _(s):
                s.wait_ge(sm["ldx"], 8 * 16)
                for k in range(ECH):
                    if k >= 2:
                        s.wait_ge(sm["sq"], 2 * k - 1)
                    slot = sqt if k % 2 == 0 else accs
                    s.activation(slot[:, :], xbuf[:, k * T:(k + 1) * T],
                                 ACT.Square).then_inc(sm["sq"], 1)
                s.wait_ge(sm["var"], 1)
                s.activation(sqt[:, :], accq[:, :], ACT.Sqrt,
                             bias=EPS).then_inc(sm["se"], 1)

            @blk.gpsimd
            def _(g):
                g.wait_ge(sm["xnm"], 1)
                g.tensor_reduce(sstat[0:1, S_XMAX:S_XMAX + 1],
                                wmx[:, WC_XNM:WC_XNM + 1], AXC, ALU.max
                                ).then_inc(sm["xmr"], 1)
                g.wait_ge(sm["xmr"], 1)
                g.dma_start(out=cc1i[:, :], in_=sstat[:, :]
                            ).then_inc(sm["cc1d"], 16)
                g.wait_ge(sm["cc1d"], 16)
                g.collective_compute(
                    "AllReduce", ALU.max, replica_groups=[list(range(8))],
                    ins=[cc1i.ap().opt()], outs=[cc1o.ap().opt()]
                ).then_inc(sm["cc1c"], 1)
            with (
                nc.psum_tensor("psA", [128, T], F32) as psA,
                nc.psum_tensor("psB", [128, T], F32) as psB,
            ):
                @blk.tensor
                def _(t):
                    t.wait_ge(sm["ldx"], 8 * 16)
                    t.wait_ge(sm["initd"], 1)
                    for k in range(ECH):
                        for nb in range(NB):
                            t.matmul(psA[:, nb * 512:(nb + 1) * 512],
                                     ones[:, :],
                                     xbuf[:, k * T + nb * 512:
                                          k * T + (nb + 1) * 512],
                                     start=(k == 0), stop=(k == ECH - 1))
                        t.wait_ge(sm["sq"], 2 * k + 1)
                        slot = sqt if k % 2 == 0 else accs
                        for nb in range(NB):
                            ins = t.matmul(psB[:, nb * 512:(nb + 1) * 512],
                                           ones[:, :],
                                           slot[:, nb * 512:(nb + 1) * 512],
                                           start=(k == 0), stop=(k == ECH - 1))
                        ins.then_inc(sm["sq"], 1)
                    t.wait_ge(sm["sq"], 2 * ECH)
                    t.sem_inc(sm["stat"], 1)

                @blk.vector
                def _(v):
                    g1 = mkguard(v, sm["vself"], vself_n)
                    v.memset(ones[:, :], 1.0)
                    v.memset(nmag[:, :], -MAGIC)
                    v.memset(pmag[:, :], MAGIC)
                    v.memset(epsb[:, :], EPS)
                    v.memset(sstat[:, :], 0.0)
                    v.memset(dsum[:, :], 0.0)
                    v.memset(sgl2[:, :], 0.0).then_inc(sm["initd"], 1)
                    v.wait_ge(sm["ldqkv"], 24 * 16)
                    v.tensor_copy(swq6[:, :], swq[:, :])
                    v.tensor_copy(swk6[:, :], swk[:, :])
                    v.tensor_copy(swv6[:, :], swv[:, :]).then_inc(sm["w16"], 1)
                    for c in range(10):
                        v.wait_ge(sm["ldw1a%d" % (c % 2)], (c // 2 + 1) * 16)
                        v.reduce_max(wmx[:, WC_W1 + c:WC_W1 + c + 1],
                                     wstage[:, (c % 2) * 2048:(c % 2 + 1) * 2048],
                                     axis=AX, apply_absolute_value=True
                                     ).then_inc(sm["vw1a"], 1)
                    v.wait_ge(sm["stat"], 1)
                    v.tensor_scalar(accs[:, :], psA[:, :], 1.0 / 1024.0, None,
                                    ALU.mult)
                    v.tensor_scalar(accq[:, :], psB[:, :], 1.0 / 1024.0, None,
                                    ALU.mult)
                    v.tensor_tensor(sqt[:, :], accs[:, :], accs[:, :], ALU.mult)
                    v.tensor_tensor(accq[:, :], accq[:, :], sqt[:, :],
                                    ALU.subtract).then_inc(sm["var"], 1)
                    v.wait_ge(sm["se"], 1)
                    v.reciprocal(accq[:, :], sqt[:, :])
                    v.scalar_tensor_tensor(accs[:, :], accs[:, :], -1.0,
                                           accq[:, :], ALU.mult, ALU.mult)
                    for k in range(ECH):
                        v.tensor_tensor(xbuf[:, k * T:(k + 1) * T],
                                        xbuf[:, k * T:(k + 1) * T],
                                        accq[:, :], ALU.mult)
                        v.tensor_tensor(xbuf[:, k * T:(k + 1) * T],
                                        xbuf[:, k * T:(k + 1) * T],
                                        accs[:, :], ALU.add
                                        ).then_inc(sm["xn"], 1)
                    for k in range(ECH):
                        ins = v.reduce_max(wmx[:, WC_XN + k:WC_XN + k + 1],
                                           xbuf[:, k * T:(k + 1) * T], axis=AX,
                                           apply_absolute_value=True)
                    g1(ins)
                    v.reduce_max(wmx[:, WC_XNM:WC_XNM + 1],
                                 wmx[:, WC_XN:WC_XN + ECH], axis=AX
                                 ).then_inc(sm["xnm"], 1)

                @blk.scalar
                def _(s):
                    for k in range(ECH):
                        s.wait_ge(sm["xn"], k + 1)
                        s.copy(hi16[:, k * T:(k + 1) * T],
                               xbuf[:, k * T:(k + 1) * T]).then_inc(sm["hi"], 1)

            with (
                nc.psum_tensor("psQ", [128, T], F32) as psQ,
                nc.psum_tensor("psV", [128, 1024], F32) as psV,
            ):
                @blk.tensor
                def _(t):
                    t.wait_ge(sm["w16"], 1)
                    for k in range(ECH):
                        t.wait_ge(sm["hi"], k + 1)
                        for nb in range(NB):
                            ins = t.matmul(
                                psQ[0:64, nb * 512:(nb + 1) * 512],
                                swq6[:, k * 64:(k + 1) * 64],
                                hi16[:, k * T + nb * 512:k * T + (nb + 1) * 512],
                                start=(k == 0), stop=(k == ECH - 1))
                    ins.then_inc(sm["qmm"], 1)

                @blk.scalar
                def _(s):
                    s.wait_ge(sm["qmm"], 1)
                    s.copy(qT16[:, :], psQ[0:64, :]).then_inc(sm["qev"], 1)

                @blk.tensor
                def _(t):
                    t.wait_ge(sm["qev"], 1)
                    for nb in range(NB):
                        for k in range(ECH):
                            ins = t.matmul(
                                psQ[0:64, nb * 512:(nb + 1) * 512],
                                swk6[:, k * 64:(k + 1) * 64],
                                hi16[:, k * T + nb * 512:k * T + (nb + 1) * 512],
                                start=(k == 0), stop=(k == ECH - 1))
                    ins.then_inc(sm["kmm"], 1)

                @blk.scalar
                def _(s):
                    s.wait_ge(sm["kmm"], 1)
                    s.copy(kT16[:, :], psQ[0:64, :]).then_inc(sm["kev"], 1)

                @blk.tensor
                def _(t):
                    for tc in range(16):
                        for k in range(ECH):
                            ins = t.matmul(
                                psV[:, tc * 64:(tc + 1) * 64],
                                hi16[:, k * T + tc * 128:k * T + (tc + 1) * 128],
                                swv6[:, k * 64:(k + 1) * 64],
                                start=(k == 0), stop=(k == ECH - 1))
                    ins.then_inc(sm["vmm"], 1)

                @blk.scalar
                def _(s):
                    s.wait_ge(sm["vmm"], 1)
                    s.copy(vnat[:, :], psV[:, :]).then_inc(sm["vev"], 1)

        with (
            nc.sbuf_tensor("hst", [128, 4 * 1024], F32) as hst,
            nc.sbuf_tensor("habs", [128, 66], F32) as habs,
            nc.sbuf_tensor("w1q", [128, ECH * HID], BF16) as w1q,
            nc.sbuf_tensor("sTt", [64, T], F32) as sTt,
            nc.sbuf_tensor("pt16", [128, 2 * T], BF16) as pt16,
            nc.sbuf_tensor("wrndG", [128, 2048], F32) as wrndG,
            nc.sbuf_tensor("wrnd", [128, 2048], F32, side="right") as wrnd,
            nc.Block() as blk,
        ):
            def w1slot(j):
                if j % 3 < 2:
                    return wstage[:, (j % 3) * 2048:(j % 3 + 1) * 2048]
                return accq[:, :]

            @blk.sync
            def _(sy):
                for c in range(12, 16):
                    sy.wait_ge(sm["vw1a"], c - 1)
                    sy.dma_start(
                        out=wstage[:, (c % 2) * 2048:(c % 2 + 1) * 2048],
                        in_=w1t[(c // 2) * 128:(c // 2) * 128 + 128,
                                (c % 2) * 2048:(c % 2 + 1) * 2048]
                    ).then_inc(sm["ldw1a%d" % (c % 2)], 16)
                for j in range(16):
                    k, half = w1_src(j)
                    if j % 3 < 2:
                        sy.wait_ge(sm["vw1a"], 15 + (j % 3))
                    else:
                        sy.wait_ge(sm["xn"], ECH)
                    if 3 <= j <= 10:
                        sy.wait_ge(sm["w1r"], j - 2)
                    elif j >= 11:
                        sy.wait_ge(sm["w1rG"], j - 10)
                    sy.dma_start(
                        out=w1slot(j),
                        in_=w1t[k * 128:(k + 1) * 128,
                                half * 2048:(half + 1) * 2048]
                    ).then_inc(sm["lw1b%d" % (j % 3)], 16)
                for c in range(16):
                    sy.wait_ge(sm["w1rG"], 8 - 2 * (c % 2))
                    if c >= 2:
                        sy.wait_ge(sm["vw2a"], c - 1)
                    sy.dma_start(
                        out=wstage.ap().rearrange("p (s q j) -> p s q j",
                                                  s=2, q=2)[:, c % 2, :, :],
                        in_=w2t[256 * c:256 * (c + 1), :].rearrange(
                            "(q p) j -> p q j", p=128)
                    ).then_inc(sm["ldw2a%d" % (c % 2)], 16)

            with (
                nc.psum_tensor("psS", [128, T], F32) as psS,
                nc.psum_tensor("psPV", [128, T], F32) as psPV,
            ):
                @blk.tensor
                def _(t):
                    t.wait_ge(sm["kev"], 1)
                    for kc in range(16):
                        if kc == 9:
                            t.wait_ge(sm["expd"], 4 * kc)
                            t.wait_ge(sm["scal1"], 1)
                            t.wait_ge(sm["sw1v"], 1)
                            t.matmul(psS[:, 0:16], ones[0:1, :], sgl2[0:1, :],
                                     start=True, stop=True
                                     ).then_inc(sm["bcAmm"], 1)
                            t.wait_ge(sm["bc1"], 1)
                        for ib in range(NB):
                            if kc > 0:
                                t.wait_ge(sm["expd"], 4 * (kc - 1) + ib + 1)
                            t.matmul(psS[:, ib * 512:(ib + 1) * 512],
                                     kT16[:, kc * 128:(kc + 1) * 128],
                                     qT16[:, ib * 512:(ib + 1) * 512],
                                     start=True, stop=True
                                     ).then_inc(sm["smm"], 1)
                        if kc > 0:
                            t.wait_ge(sm["vs"], kc)
                            for ib in range(NB):
                                ins = t.matmul(
                                    psPV[0:64, ib * 512:(ib + 1) * 512],
                                    vs16[:, (kc - 1) * 64:kc * 64],
                                    pt16[:, ((kc - 1) % 2) * T + ib * 512:
                                         ((kc - 1) % 2) * T + (ib + 1) * 512],
                                    start=(kc == 1), stop=False)
                            ins.then_inc(sm["pv"], 1)
                    t.wait_ge(sm["vs"], 16)
                    for ib in range(NB):
                        ins = t.matmul(psPV[0:64, ib * 512:(ib + 1) * 512],
                                       vs16[:, 15 * 64:16 * 64],
                                       pt16[:, T + ib * 512:T + (ib + 1) * 512],
                                       start=False, stop=True)
                    ins.then_inc(sm["pv"], 1)

                @blk.vector
                def _(v):
                    g1 = mkguard(v, sm["vself"], vself_n)
                    g2 = mkguard(v, sm["vself2"], vself2_n)

                    def attn(kc):
                        v.wait_ge(sm["expd"], 4 * (kc + 1))
                        if kc == 0:
                            v.wait_ge(sm["vev"], 1)
                        g1(v.reduce_sum(dsum[:, kc:kc + 1],
                                        pt16[:, (kc % 2) * T:(kc % 2 + 1) * T],
                                        axis=AX))
                        g1(v.reciprocal(dinv[:, kc:kc + 1], dsum[:, kc:kc + 1]))
                        v.tensor_scalar(vs16[:, kc * 64:(kc + 1) * 64],
                                        vnat[:, kc * 64:(kc + 1) * 64],
                                        dinv[:, kc:kc + 1], None, ALU.mult
                                        ).then_inc(sm["vs"], 1)

                    def w1a(c):
                        v.wait_ge(sm["ldw1a%d" % (c % 2)], (c // 2 + 1) * 16)
                        v.reduce_max(wmx[:, WC_W1 + c:WC_W1 + c + 1],
                                     wstage[:, (c % 2) * 2048:(c % 2 + 1) * 2048],
                                     axis=AX, apply_absolute_value=True
                                     ).then_inc(sm["vw1a"], 1)

                    def xq1(k):
                        if k >= 1:
                            v.wait_ge(sm["xq"], k)
                        v.tensor_scalar(sqt[:, :],
                                        xbuf[:, k * T:(k + 1) * T],
                                        scal[:, C_INV1:C_INV1 + 1], MAGIC,
                                        ALU.mult, ALU.add
                                        ).then_inc(sm["vacc"], 1)

                    def w1q1(c):
                        v.wait_ge(sm["lw1b%d" % (c % 3)], (c // 3 + 1) * 16)
                        if c >= 3:
                            v.wait_ge(sm["w1q"], c - 2)
                        wr = (wrnd[:, :] if c % 3 == 0 else
                              hst[:, (c % 3 - 1) * 2048:(c % 3) * 2048])
                        v.tensor_scalar(wr, w1slot(c),
                                        scal[:, C_INVW1:C_INVW1 + 1], MAGIC,
                                        ALU.mult, ALU.add
                                        ).then_inc(sm["w1r"], 1)

                    attn(0)
                    w1a(10); w1a(11)
                    attn(1)
                    w1a(12); w1a(13)
                    attn(2)
                    w1a(14); w1a(15)
                    v.wait_ge(sm["vw1a"], 16)
                    v.reduce_max(wmx[:, WC_W1M:WC_W1M + 1],
                                 wmx[:, WC_W1:WC_W1 + 16], axis=AX
                                 ).then_inc(sm["vw1a"], 1)
                    attn(3); attn(4); attn(5)
                    v.wait_ge(sm["w1mg"], 1)
                    g2(v.tensor_scalar(sgl2[0:1, 9:10],
                                       sstat[0:1, S_W1MAX:S_W1MAX + 1],
                                       1e-6, INV127, ALU.max, ALU.mult))
                    g2(v.reciprocal(sgl2[0:1, C_INVW1:C_INVW1 + 1],
                                    sgl2[0:1, 9:10]))
                    v.sem_inc(sm["sw1v"], 1)
                    attn(6); attn(7); attn(8)
                    v.wait_ge(sm["cc1d"], 32)
                    g2(v.tensor_scalar(sgl2[0:1, 8:9],
                                       sgl[0:1, S_XMAX:S_XMAX + 1],
                                       1e-6, INV127, ALU.max, ALU.mult))
                    g2(v.reciprocal(sgl2[0:1, C_INV1:C_INV1 + 1],
                                    sgl2[0:1, 8:9]))
                    g2(v.tensor_tensor(sgl2[0:1, C_C1:C_C1 + 1], sgl2[0:1, 8:9],
                                       sgl2[0:1, 9:10], ALU.mult))
                    v.sem_inc(sm["scal1"], 1)
                    attn(9)
                    v.wait_ge(sm["bc1"], 1)
                    v.tensor_scalar(b1c[:, :], sb1[:, :],
                                    scal[:, C_C1:C_C1 + 1], None, ALU.mult)
                    attn(10)
                    xq1(0)
                    w1q1(0)
                    attn(11)
                    xq1(1); xq1(2)
                    w1q1(1)
                    attn(12)
                    xq1(3); xq1(4)
                    w1q1(2)
                    attn(13)
                    xq1(5); xq1(6)
                    w1q1(3)
                    attn(14)
                    xq1(7)
                    w1q1(4)
                    attn(15)
                    for c in range(5, 8):
                        w1q1(c)

                @blk.scalar
                def _(s):
                    def expk(kc):
                        if kc >= 2:
                            s.wait_ge(sm["pv"], kc - 1)
                        for ib in range(NB):
                            s.wait_ge(sm["smm"], 4 * kc + ib + 1)
                            s.activation(
                                pt16[:, (kc % 2) * T + ib * 512:
                                     (kc % 2) * T + (ib + 1) * 512],
                                psS[:, ib * 512:(ib + 1) * 512], ACT.Exp
                            ).then_inc(sm["expd"], 1)

                    def xq2(k):
                        s.wait_ge(sm["vacc"], k + 1)
                        s.activation(xq16[:, k * T:(k + 1) * T], sqt[:, :],
                                     ACT.Copy, bias=-MAGIC
                                     ).then_inc(sm["xq"], 1)

                    def w1q2(c):
                        k, half = w1_src(c)
                        if c < 8:
                            s.wait_ge(sm["w1r"], c + 1)
                            wr = (wrnd[:, :] if c % 3 == 0 else
                                  hst[:, (c % 3 - 1) * 2048:(c % 3) * 2048])
                        else:
                            s.wait_ge(sm["w1rG"], c - 7)
                            wr = wrndG[:, :]
                        s.wait_ge(sm["vacc"], min(k + 3 + half, ECH))
                        s.activation(
                            w1q[:, k * HID + half * 2048:
                                k * HID + (half + 1) * 2048],
                            wr,
                            ACT.Copy, bias=-MAGIC
                        ).then_inc(sm["w1q"], 1)

                    s.wait_ge(sm["vmm"], 1)
                    for kc in range(9):
                        expk(kc)
                    s.wait_ge(sm["bcAmm"], 1)
                    s.wait_ge(sm["ldb"], 32)
                    s.copy(scal[:, 0:3], psS[:, 0:3]).then_inc(sm["bc1"], 1)
                    expk(9); expk(10)
                    xq2(0)
                    expk(11)
                    xq2(1); xq2(2)
                    w1q2(0)
                    expk(12)
                    xq2(3); xq2(4)
                    w1q2(1)
                    expk(13)
                    xq2(5); xq2(6)
                    w1q2(2)
                    expk(14)
                    xq2(7)
                    w1q2(3)
                    expk(15)
                    for c in range(4, 8):
                        w1q2(c)
                    s.wait_ge(sm["pv"], 16)
                    s.copy(sTt[:, :], psPV[0:64, :]).then_inc(sm["sev"], 1)

                @blk.gpsimd
                def _(g):
                    g.wait_ge(sm["vw1a"], 17)
                    g.tensor_reduce(sstat[0:1, S_W1MAX:S_W1MAX + 1],
                                    wmx[:, WC_W1M:WC_W1M + 1], AXC, ALU.max
                                    ).then_inc(sm["w1mg"], 1)
                    g.wait_ge(sm["cc1c"], 1)
                    g.dma_start(out=sgl[:, :], in_=cc1o[:, :]
                                ).then_inc(sm["cc1d"], 16)
                    g.wait_ge(sm["sev"], 1)
                    g.dma_start(out=s2b[0:64, :], in_=sTt[:, :]
                                ).then_inc(sm["s2bd"], 16)
                    g.dma_start(out=s2b[64:128, :], in_=sTt[:, :]
                                ).then_inc(sm["s2bd"], 16)

            with nc.psum_tensor("psF", [128, 4 * 1024], F32) as psF:
                @blk.tensor
                def _(t):
                    t.wait_ge(sm["xq"], ECH)
                    for hm in range(HM2):
                        m, half = hm // 2, hm % 2
                        if hm == 0:
                            t.wait_ge(sm["expd"], 64)
                            t.wait_ge(sm["w1q"], 8)
                        if hm == 2:
                            t.wait_ge(sm["sev"], 1)
                        if hm == 32:
                            t.wait_ge(sm["w1q"], 16)
                        if hm >= 4:
                            t.wait_ge(sm["gel"], hm - 3)
                        pf = psF[:, (hm % 4) * 1024:(hm % 4 + 1) * 1024]
                        for nb2 in range(2):
                            for k in range(ECH):
                                ins = t.matmul(
                                    pf[:, nb2 * 512:(nb2 + 1) * 512],
                                    w1q[:, k * HID + m * 128:
                                        k * HID + (m + 1) * 128],
                                    xq16[:, k * T + half * 1024 + nb2 * 512:
                                         k * T + half * 1024 + (nb2 + 1) * 512],
                                    start=(k == 0), stop=(k == ECH - 1))
                        ins.then_inc(sm["f1"], 1)
                    t.wait_ge(sm["gel"], HM2)
                    t.wait_ge(sm["sw2v"], 1)
                    t.matmul(psF[:, 0:16], ones[0:1, :], sgl2[0:1, :],
                             start=True, stop=True).then_inc(sm["bcWmm"], 1)
                    t.wait_ge(sm["scal2"], 1)
                    t.matmul(psF[:, 1024:1040], ones[0:1, :], sgl2[0:1, :],
                             start=True, stop=True).then_inc(sm["bc2mm"], 1)

                @blk.scalar
                def _(s):
                    def w1q2G(c):
                        k, half = w1_src(c)
                        s.wait_ge(sm["w1rG"], c - 7)
                        s.activation(
                            w1q[:, k * HID + half * 2048:
                                k * HID + (half + 1) * 2048],
                            wrndG[:, :],
                            ACT.Copy, bias=-MAGIC
                        ).then_inc(sm["w1q"], 1)

                    for hm in range(HM2):
                        m = hm // 2
                        s.wait_ge(sm["f1"], hm + 1)
                        if hm >= 4:
                            s.wait_ge(sm["spl%d" % (hm % 4)],
                                      (hm // 4) * 16)
                            s.wait_ge(sm["habs"], hm - 3)
                        pf = psF[:, (hm % 4) * 1024:(hm % 4 + 1) * 1024]
                        hs = hst[:, (hm % 4) * 1024:(hm % 4 + 1) * 1024]
                        for nb2 in range(2):
                            ins = s.activation(hs[:, nb2 * 512:(nb2 + 1) * 512],
                                               pf[:, nb2 * 512:(nb2 + 1) * 512],
                                               ACT.Gelu, bias=b1c[:, m:m + 1],
                                               scale=scal[:, C_C1:C_C1 + 1])
                        ins.then_inc(sm["gel"], 1)
                        if hm % 2 == 1 and hm < 16:
                            w1q2G(8 + hm // 2)
                    s.wait_ge(sm["bcWmm"], 1)
                    s.copy(scal[:, C_INVW2:C_INVW2 + 1],
                           psF[:, 3:4]).then_inc(sm["bcw"], 1)
                    s.wait_ge(sm["vw2a"], 17)
                    for c in range(4):
                        s.dma_start(
                            out=wstage[:, c * 1024:(c + 1) * 1024],
                            in_=w2t[c * 128:(c + 1) * 128, :]
                        ).then_inc(sm["lw2b%d" % c], 16)
                    s.wait_ge(sm["bc2mm"], 1)
                    s.copy(scal[:, C_INV2:C_C2 + 1],
                           psF[:, 1024 + 4:1024 + 6]).then_inc(sm["bc2"], 1)

                @blk.vector
                def _(v):
                    g1 = mkguard(v, sm["vself"], vself_n)
                    g2 = mkguard(v, sm["vself2"], vself2_n)
                    for hm in range(HM2):
                        v.wait_ge(sm["gel"], hm + 1)
                        ins = v.reduce_max(habs[:, hm:hm + 1],
                                           hst[:, (hm % 4) * 1024:
                                               (hm % 4 + 1) * 1024],
                                           axis=AX, apply_absolute_value=True)
                        ins.then_inc(sm["habs"], 1)
                        if 16 <= hm < 48 and hm % 2 == 0:
                            c = (hm - 16) // 2
                            v.wait_ge(sm["ldw2a%d" % (c % 2)],
                                      (c // 2 + 1) * 16)
                            v.reduce_max(
                                wmx[:, WC_W2 + c:WC_W2 + c + 1],
                                wstage[:, (c % 2) * 2048:(c % 2 + 1) * 2048],
                                axis=AX, apply_absolute_value=True
                            ).then_inc(sm["vw2a"], 1)
                        if hm == 49:
                            v.wait_ge(sm["vw2a"], 16)
                            v.reduce_max(wmx[:, WC_W2M:WC_W2M + 1],
                                         wmx[:, WC_W2:WC_W2 + 16], axis=AX
                                         ).then_inc(sm["vw2a"], 1)
                        if hm == 52:
                            v.wait_ge(sm["w2mg"], 1)
                            g2(v.tensor_scalar(sgl2[0:1, 10:11],
                                               sstat[0:1, S_W2MAX:S_W2MAX + 1],
                                               1e-6, INV127, ALU.max, ALU.mult))
                            g2(v.reciprocal(sgl2[0:1, C_INVW2:C_INVW2 + 1],
                                            sgl2[0:1, 10:11]))
                            v.sem_inc(sm["sw2v"], 1)
                    v.wait_ge(sm["habs"], HM2)
                    v.reduce_max(habs[:, 64:65], habs[:, 0:64], axis=AX
                                 ).then_inc(sm["habs"], 1)
                    v.wait_ge(sm["cc2d"], 32)
                    g2(v.tensor_scalar(sgl2[0:1, 11:12],
                                       sgl[0:1, S_HMAX:S_HMAX + 1],
                                       1e-6, INV127, ALU.max, ALU.mult))
                    g2(v.reciprocal(sgl2[0:1, C_INV2:C_INV2 + 1],
                                    sgl2[0:1, 11:12]))
                    g2(v.tensor_tensor(sgl2[0:1, C_C2:C_C2 + 1],
                                       sgl2[0:1, 11:12],
                                       sgl2[0:1, 10:11], ALU.mult))
                    v.sem_inc(sm["scal2"], 1)
                    v.wait_ge(sm["bc2"], 1)
                    v.tensor_scalar(b2c[:, :], sb2[:, :],
                                    scal[:, C_C2:C_C2 + 1], None, ALU.mult
                                    ).then_inc(sm["b2c"], 1)

                @blk.gpsimd
                def _(g):
                    def w1qG(c):
                        g.wait_ge(sm["lw1b%d" % (c % 3)], (c // 3 + 1) * 16)
                        if c >= 9:
                            g.wait_ge(sm["w1q"], c)
                        ws = w1slot(c)
                        g.tensor_scalar(wrndG[:, 0:1024], ws[:, 0:1024],
                                        scal[:, C_INVW1:C_INVW1 + 1], MAGIC,
                                        ALU.mult, ALU.add)
                        g.tensor_scalar(wrndG[:, 1024:2048], ws[:, 1024:2048],
                                        scal[:, C_INVW1:C_INVW1 + 1], MAGIC,
                                        ALU.mult, ALU.add
                                        ).then_inc(sm["w1rG"], 1)

                    for hm in range(HM2):
                        m, half = hm // 2, hm % 2
                        g.wait_ge(sm["gel"], hm + 1)
                        g.dma_start(out=hsp[m * 128:(m + 1) * 128,
                                            half * 1024:(half + 1) * 1024],
                                    in_=hst[:, (hm % 4) * 1024:
                                            (hm % 4 + 1) * 1024]
                                    ).then_inc(sm["spl%d" % (hm % 4)], 16)
                        if hm % 2 == 0 and hm < 16:
                            w1qG(8 + hm // 2)
                        if hm == 50:
                            g.wait_ge(sm["vw2a"], 17)
                            g.tensor_reduce(sstat[0:1, S_W2MAX:S_W2MAX + 1],
                                            wmx[:, WC_W2M:WC_W2M + 1], AXC,
                                            ALU.max).then_inc(sm["w2mg"], 1)
                    g.wait_ge(sm["habs"], HM2 + 1)
                    g.tensor_reduce(sstat[0:1, S_HMAX:S_HMAX + 1],
                                    habs[:, 64:65], AXC, ALU.max
                                    ).then_inc(sm["hmg"], 1)
                    g.wait_ge(sm["hmg"], 1)
                    g.dma_start(out=cc2i[:, :], in_=sstat[:, :]
                                ).then_inc(sm["cc2d"], 16)
                    g.wait_ge(sm["cc2d"], 16)
                    g.collective_compute(
                        "AllReduce", ALU.max, replica_groups=[list(range(8))],
                        ins=[cc2i.ap().opt()], outs=[cc2o.ap().opt()]
                    ).then_inc(sm["cc2c"], 1)
                    g.wait_ge(sm["cc2c"], 1)
                    g.dma_start(out=sgl[:, :], in_=cc2o[:, :]
                                ).then_inc(sm["cc2d"], 16)

        xq_es.close()

        with (
            nc.sbuf_tensor("w2q", [128, HCH * 1024], BF16) as w2q,
            nc.sbuf_tensor("hq16", [128, HCH * TBW], BF16) as hq16,
            nc.sbuf_tensor("gbuf", [128, 8 * TBW], F32) as gbuf,
            nc.sbuf_tensor("hrd", [128, 8 * TBW], F32) as hrd,
            nc.sbuf_tensor("xrld", [128, 8 * TBW], F32) as xrld,
            nc.sbuf_tensor("wrnd2", [128, 2048], F32, side="right") as wrnd2,
            nc.Block() as blk,
        ):
            def spl_wait(eng, tb, k):
                hm = 2 * k + tb // 2
                eng.wait_ge(sm["spl%d" % (hm % 4)], (hm // 4 + 1) * 16)

            @blk.gpsimd
            def _(g):
                g.wait_ge(sm["bcw"], 1)
                g.wait_ge(sm["f1"], HM2)
                odds = [i for i in range(TBN * HCH) if i % 2 == 1]

                def odd_ld(j):
                    jtb, jk = divmod(j, HCH)
                    spl_wait(g, jtb, jk)
                    if j >= 8:
                        g.wait_ge(sm["hq"], j - 7)
                    g.dma_start(
                        out=hrd[:, (j % 8) * TBW:(j % 8 + 1) * TBW],
                        in_=hsp[jk * 128:(jk + 1) * 128,
                                jtb * TBW:(jtb + 1) * TBW]
                    ).then_inc(sm["ldh%d" % (j % 8)], 16)

                nxt = 0
                for c in range(HCH):
                    g.wait_ge(sm["lw2b%d" % (c % 4)], (c // 4 + 1) * 16)
                    if c >= 2:
                        g.wait_ge(sm["w2q"], c - 1)
                    g.tensor_scalar(wrnd2[:, (c % 2) * 1024:(c % 2 + 1) * 1024],
                                    wstage[:, (c % 4) * 1024:(c % 4 + 1) * 1024],
                                    scal[:, C_INVW2:C_INVW2 + 1], MAGIC,
                                    ALU.mult, ALU.add
                                    ).then_inc(sm["hmg"], 1)
                    while nxt < len(odds) and odds[nxt] <= c + 8:
                        odd_ld(odds[nxt])
                        nxt += 1
                while nxt < len(odds):
                    odd_ld(odds[nxt])
                    nxt += 1

            with nc.psum_tensor("psO", [128, 8 * 512], F32) as psO:
                @blk.sync
                def _(sy):
                    for tb in range(TBN):
                        for k in range(HCH):
                            idx = tb * HCH + k
                            if idx % 2 == 1:
                                continue
                            spl_wait(sy, tb, k)
                            if idx >= 8:
                                sy.wait_ge(sm["hq"], idx - 7)
                            sy.dma_start(
                                out=hrd[:, (idx % 8) * TBW:(idx % 8 + 1) * TBW],
                                in_=hsp[k * 128:(k + 1) * 128,
                                        tb * TBW:(tb + 1) * TBW]
                            ).then_inc(sm["ldh%d" % (idx % 8)], 16)
                        if tb > 0:
                            sy.wait_ge(sm["ep"], tb * 8)
                        for g_ in range(8):
                            sy.dma_start(
                                out=xrld[:, g_ * TBW:(g_ + 1) * TBW],
                                in_=xT[g_ * 128:(g_ + 1) * 128,
                                       tb * TBW:(tb + 1) * TBW]
                            ).then_inc(sm["ldxr"], 16)

                def emit_hq(v, tb, k0, k1):
                    for k in range(k0, k1):
                        idx = tb * HCH + k
                        v.wait_ge(sm["ldh%d" % (idx % 8)], (idx // 8 + 1) * 16)
                        if idx >= 8:
                            v.wait_ge(sm["hq"], idx - 7)
                        hr = hrd[:, (idx % 8) * TBW:(idx % 8 + 1) * TBW]
                        v.tensor_scalar(hr, hr, scal[:, C_INV2:C_INV2 + 1],
                                        MAGIC, ALU.mult, ALU.add
                                        ).then_inc(sm["hq1"], 1)

                def emit_ep(v, tb, g0, g1):
                    if g0 == 0:
                        v.wait_ge(sm["ldxr"], (tb + 1) * 8 * 16)
                    for g_ in range(g0, g1):
                        v.wait_ge(sm["sil"], tb * 8 + g_ + 1)
                        v.tensor_tensor(gbuf[:, g_ * TBW:(g_ + 1) * TBW],
                                        gbuf[:, g_ * TBW:(g_ + 1) * TBW],
                                        s2b[:, tb * TBW:(tb + 1) * TBW],
                                        ALU.mult)
                        v.tensor_tensor(gbuf[:, g_ * TBW:(g_ + 1) * TBW],
                                        gbuf[:, g_ * TBW:(g_ + 1) * TBW],
                                        xrld[:, g_ * TBW:(g_ + 1) * TBW],
                                        ALU.add).then_inc(sm["ep"], 1)

                @blk.vector
                def _(v):
                    v.wait_ge(sm["b2c"], 1)
                    v.wait_ge(sm["s2bd"], 32)
                    emit_hq(v, 0, 0, HCH)
                    for tb in range(1, TBN):
                        emit_hq(v, tb, 0, 8)
                        emit_ep(v, tb - 1, 0, 4)
                        emit_hq(v, tb, 8, 16)
                        emit_ep(v, tb - 1, 4, 8)
                        emit_hq(v, tb, 16, HCH)
                    emit_ep(v, TBN - 1, 0, 8)

                @blk.tensor
                def _(t):
                    for tb in range(TBN):
                        for k in range(HCH):
                            t.wait_ge(sm["hq"], tb * HCH + k + 1)
                            if tb == 0:
                                t.wait_ge(sm["w2q"], k + 1)
                            for g_ in range(8):
                                if k == 0 and tb > 0:
                                    t.wait_ge(sm["sil"], (tb - 1) * 8 + g_ + 1)
                                ins = t.matmul(
                                    psO[:, g_ * 512:(g_ + 1) * 512],
                                    w2q[:, k * 1024 + g_ * 128:
                                        k * 1024 + (g_ + 1) * 128],
                                    hq16[:, k * TBW:(k + 1) * TBW],
                                    start=(k == 0), stop=(k == HCH - 1))
                            ins.then_inc(sm["mmk"], 1)

                def emit_hq2(s, tb, k0, k1):
                    for k in range(k0, k1):
                        idx = tb * HCH + k
                        s.wait_ge(sm["hq1"], idx + 1)
                        if tb > 0:
                            s.wait_ge(sm["mmk"], (tb - 1) * HCH + k + 1)
                        s.activation(hq16[:, k * TBW:(k + 1) * TBW],
                                     hrd[:, (idx % 8) * TBW:(idx % 8 + 1) * TBW],
                                     ACT.Copy, bias=-MAGIC
                                     ).then_inc(sm["hq"], 1)

                def emit_silu(s, tb):
                    s.wait_ge(sm["mmk"], (tb + 1) * HCH)
                    for g_ in range(8):
                        if tb > 0:
                            s.wait_ge(sm["outd%d" % g_], tb * 16)
                        s.activation(gbuf[:, g_ * TBW:(g_ + 1) * TBW],
                                     psO[:, g_ * 512:(g_ + 1) * 512],
                                     ACT.Silu, bias=b2c[:, g_:g_ + 1],
                                     scale=scal[:, C_C2:C_C2 + 1]
                                     ).then_inc(sm["sil"], 1)

                def emit_outs(s, tb):
                    for g_ in range(8):
                        s.wait_ge(sm["ep"], tb * 8 + g_ + 1)
                        s.dma_start(
                            out=outT[g_ * 128:(g_ + 1) * 128,
                                     tb * TBW:(tb + 1) * TBW],
                            in_=gbuf[:, g_ * TBW:(g_ + 1) * TBW]
                        ).then_inc(sm["outd%d" % g_], 16)

                @blk.scalar
                def _(s):
                    for i in range(4):
                        s.wait_ge(sm["spl%d" % i], 16 * 16)
                    s.wait_ge(sm["b2c"], 1)
                    for k in range(HCH):
                        s.wait_ge(sm["hmg"], k + 2)
                        if k + 4 < HCH:
                            s.dma_start(
                                out=wstage[:, ((k + 4) % 4) * 1024:
                                           ((k + 4) % 4 + 1) * 1024],
                                in_=w2t[(k + 4) * 128:(k + 5) * 128, :]
                            ).then_inc(sm["lw2b%d" % ((k + 4) % 4)], 16)
                        s.activation(w2q[:, k * 1024:(k + 1) * 1024],
                                     wrnd2[:, (k % 2) * 1024:(k % 2 + 1) * 1024],
                                     ACT.Copy, bias=-MAGIC
                                     ).then_inc(sm["w2q"], 1)
                        s.wait_ge(sm["hq1"], k + 1)
                        s.activation(hq16[:, k * TBW:(k + 1) * TBW],
                                     hrd[:, (k % 8) * TBW:(k % 8 + 1) * TBW],
                                     ACT.Copy, bias=-MAGIC
                                     ).then_inc(sm["hq"], 1)
                    for tb in range(1, TBN):
                        emit_hq2(s, tb, 0, 8)
                        emit_silu(s, tb - 1)
                        emit_outs(s, tb - 1)
                        emit_hq2(s, tb, 8, HCH)
                    emit_silu(s, TBN - 1)
                    emit_outs(s, TBN - 1)
                    for g_ in range(8):
                        s.wait_ge(sm["outd%d" % g_], TBN * 16)

    return nc


def kernel(L0_attn_x, L0_attn_Wq, L0_attn_Wk, L0_attn_Wv,
           L0_ffn_w1, L0_ffn_b1, L0_ffn_w2, L0_ffn_b2):
    x = np.asarray(L0_attn_x, dtype=np.float32)
    wq = np.ascontiguousarray(np.asarray(L0_attn_Wq, dtype=np.float32))
    wk = np.ascontiguousarray(np.asarray(L0_attn_Wk, dtype=np.float32))
    wv = np.ascontiguousarray(np.asarray(L0_attn_Wv, dtype=np.float32))
    w1t = np.ascontiguousarray(np.asarray(L0_ffn_w1, dtype=np.float32).T)
    w2t = np.ascontiguousarray(np.asarray(L0_ffn_w2, dtype=np.float32).T)
    b1r = np.ascontiguousarray(
        np.asarray(L0_ffn_b1, dtype=np.float32).reshape(32, 128).T)
    b2r = np.ascontiguousarray(
        np.asarray(L0_ffn_b2, dtype=np.float32).reshape(8, 128).T)

    nc = build_nc()
    in_maps = []
    for b in range(8):
        in_maps.append(dict(xT=np.ascontiguousarray(x[b].T), wq=wq, wk=wk,
                            wv=wv, w1t=w1t, w2t=w2t, b1r=b1r, b2r=b2r))
    res = run_bass_kernel_spmd(nc, in_maps, core_ids=list(range(8)))
    return np.stack([np.ascontiguousarray(r["outT"].T) for r in res.results])
